# version 50
# baseline (speedup 1.0000x reference)
import contextlib
import numpy as np
import concourse.bass as bass
import concourse.mybir as mybir
from concourse.bass_utils import run_bass_kernel_spmd

F32 = mybir.dt.float32
BF16 = mybir.dt.bfloat16
AF = mybir.ActivationFunctionType
ALU = mybir.AluOpType

D = 1024
NB = 8
NT = 16
N = 256
SEQ = NT * N
EPS = 1e-5

SLABS = [(w, ke) for ke in range(NB) for w in (0, 1, 2)] \
      + [(w, ke) for ke in range(NB) for w in (3, 4)] \
      + [(5, ke) for ke in range(NB)]
NSLAB = len(SLABS)


def build_nc(shadow=None):
    nc = bass.Bass("TRN2", target_bir_lowering=False)

    xT = nc.dram_tensor("xT", [D, SEQ], F32, kind="ExternalInput").ap()
    wnames = ["wg", "wv", "wd", "wr", "wk", "wvf"]
    wsrc = [nc.dram_tensor(n, [128, NB * 1024], F32, kind="ExternalInput").ap()
            for n in wnames]
    bpk_d = nc.dram_tensor("bpk", [128, 40], F32, kind="ExternalInput").ap()
    yT = nc.dram_tensor("yT", [D, SEQ], F32, kind="ExternalOutput").ap()
    xTr = xT.rearrange("(kd p) t -> p kd t", p=128)
    yTr = yT.rearrange("(kd p) t -> p kd t", p=128)

    es = contextlib.ExitStack()
    with es:
        def sb(name, shape, dt):
            return es.enter_context(nc.sbuf_tensor(name, shape, dt))
        wb = [sb(f"wb{n}", [128, NB * 1024], BF16) for n in wnames]
        stage = sb("stage", [128, 2048], F32)
        xbf = sb("xbf", [128, 2 * 2048], BF16)
        sg = sb("sg", [128, 512], F32)
        tv = sb("tv", [128, 512], F32)
        sd = sb("sd", [128, 512], F32)
        xs = sb("xs", [128, 2048], F32)
        aa = sb("aa", [128, 2048], F32)
        zz = [sb("z0", [128, 2048], F32), sb("z1", [128, 2048], F32)]
        carry = sb("carry", [128, NB], F32)
        xnew = sb("xnew", [128, 2048], F32)
        y1 = sb("y1", [128, NB * (N + 1)], F32)
        dbuf = sb("dbuf", [128, 512], F32)
        xrk = sb("xrk", [128, 4096], BF16)
        rbuf = sb("rbuf", [128, 2048], F32)
        rk = sb("rk", [128, 512], F32)
        kab = sb("kab", [128, 2048], BF16)
        y2 = sb("y2", [128, 2048], F32)
        bpk = sb("bpk_s", [128, 40], F32)
        ones_col = sb("ones_col", [128, 1], BF16)
        ones_rowf = sb("ones_rowf", [1, 128], F32)
        row_mq = sb("row_mq", [1, 2 * N], F32)
        row_t = sb("row_t", [1, N], F32)
        row_var = sb("row_var", [1, N], F32)
        row_inv = sb("row_inv", [1, N], F32)
        consts = sb("consts", [128, 4], F32)

        def ps(name):
            return es.enter_context(nc.psum_tensor(name, [128, 512], F32))
        pg, pv, pd = ps("pg"), ps("pv"), ps("pd")
        pkr, pkk = ps("pkr"), ps("pkk")
        def f1b(ke):
            return pkr if ke % 2 == 0 else pkk
        po = [ps("po0"), ps("po1")]
        psb = ps("psb")

        def sem(name):
            return es.enter_context(nc.semaphore(name))
        s_bld, s_xf, s_xst, s_yst = sem("bld"), sem("xf"), sem("xst"), sem("yst")
        s_xst1 = sem("xst1")
        s_wld = [sem(f"wld{k}") for k in range(9)]
        s_misc, s_wcast, s_xbf = sem("misc"), sem("wcast"), sem("xbf")
        s_amm, s_aevac, s_attnv, s_scan = sem("amm"), sem("aevac"), sem("attnv"), sem("scan")
        s_aav = sem("aav")
        s_z, s_zb1, s_zb2, s_stats = sem("z"), sem("zb1"), sem("zb2"), sem("stats")
        s_rv, s_rs, s_rpsum = sem("rv"), sem("rs"), sem("rpsum")
        s_bcast, s_pass, s_d, s_xrk = sem("bcast"), sem("pass"), sem("d"), sem("xrk")
        s_f1mm, s_f1ev, s_f2mm, s_ov = sem("f1mm"), sem("f1ev"), sem("f2mm"), sem("ov")
        s_z2, s_y2 = sem("z2"), sem("y2")
        s_ordv, s_ords, s_ordg = sem("ordv"), sem("ords"), sem("ordg")
        s_cw = sem("cw")
        s_pcp = sem("pcp")
        ordv_n = [0]
        ords_n = [0]
        ordg_n = [0]

        block = es.enter_context(nc.Block())

        def ev(engine, kind, s, v):
            if shadow is not None:
                shadow[engine].append((kind, s.name, v))

        def W(eng, name, s, v):
            if v <= 0:
                return
            eng.wait_ge(s, v)
            ev(name, "wait", s, v)

        def INC(instr, name, s, v):
            instr.then_inc(s, v)
            ev(name, "inc", s, v)
            return instr

        def vfence(v, ins):
            ordv_n[0] += 1
            INC(ins, "vec", s_ordv, 1)
            W(v, "vec", s_ordv, ordv_n[0])

        def gfence(g, ins):
            ordg_n[0] += 1
            INC(ins, "pool", s_ordg, 1)
            W(g, "pool", s_ordg, ordg_n[0])

        def sfence(sc, ins):
            ords_n[0] += 1
            INC(ins, "sc", s_ords, 1)
            W(sc, "sc", s_ords, ords_n[0])

        def half(t, par):
            return t[:, par * N:(par + 1) * N]
        def blk(t, k):
            return t[:, k * N:(k + 1) * N]
        def Z(i):
            return zz[i % 2]
        def z3(i):
            return bass.AP(Z(i), 0, [[2048, 128], [N, NB], [1, N]])
        def wtile(w, ke, kd):
            return wb[w][:, (ke * 8 + kd) * 128:(ke * 8 + kd) * 128 + 128]
        def xbf_blk(j, kd):
            off = (j % 2) * 2048 + kd * N
            return xbf[:, off:off + N]
        y2_3a = bass.AP(y2, 0, [[2048, 128], [N, 4], [1, N]])
        y2_3b = bass.AP(y2, 4 * N, [[2048, 128], [N, 4], [1, N]])
        def z3h(j, h):
            return bass.AP(Z(j), h * 4 * N, [[2048, 128], [N, 4], [1, N]])
        y1w = N + 1
        y1_margin = bass.AP(y1, 1, [[NB * y1w, 128], [y1w, NB], [1, N]])
        def y1v(ke):
            return bass.AP(y1, ke * y1w + 1, [[NB * y1w, 128], [1, N]])
        def xpv(ke):
            return bass.AP(y1, ke * y1w, [[NB * y1w, 128], [1, N]])
        y1_c0 = bass.AP(y1, 0, [[NB * y1w, 128], [y1w, NB]])
        y1_cN = bass.AP(y1, N, [[NB * y1w, 128], [y1w, NB]])
        Mb_v = bass.AP(psb, 0, [[512, 128], [0, NB], [1, N]])
        Sb_v = bass.AP(psb, N, [[512, 128], [0, NB], [1, N]])
        Mb_d4 = bass.AP(dbuf, 0, [[512, 128], [0, 4], [1, N]])
        Sb_d4 = bass.AP(dbuf, N, [[512, 128], [0, 4], [1, N]])
        p_sum = psb[0:1, 0:N]
        p_sq = psb[0:1, N:2 * N]
        def xr_blk(k):
            return xrk[:, k * 512:k * 512 + 256]
        def xk_blk(k):
            return xrk[:, k * 512 + 256:(k + 1) * 512]
        def xrk_pair(k):
            return xrk[:, k * 512:(k + 1) * 512]
        def zb1_blk(k):
            return bass.AP(rbuf, k * 256, [[2048, 128], [1, 128]]).bitcast(BF16)
        def zq1_blk(k):
            return bass.AP(rbuf, k * 256 + 128, [[2048, 128], [1, 128]]).bitcast(BF16)
        def zz1_pair(k):
            return bass.AP(rbuf, k * 256, [[2048, 128], [1, 256]]).bitcast(BF16)
        y1v_full = bass.AP(y1, 1, [[NB * y1w, 128], [y1w, NB], [1, N]])
        xpv_full = bass.AP(y1, 0, [[NB * y1w, 128], [y1w, NB], [1, N]])
        rbuf_blks = bass.AP(rbuf, 0, [[2048, 128], [N, NB], [1, N]])

        def x_dma(q, qname, dst_sb, ti, lo, hi, s, n16):
            dst = dst_sb[:, :].rearrange("p (kd n) -> p kd n", kd=NB)[:, lo:hi, :]
            src = xTr[:, lo:hi, ti * N:(ti + 1) * N]
            INC(q.dma_start(out=dst, in_=src), qname, s, n16)

        xrk_f32 = xrk[:, :].bitcast(F32)
        kab_f32 = kab[:, :].bitcast(F32)

        def wslot(s):
            r = s % 9
            if r == 0:
                return stage[:, 0:1024]
            if r == 1:
                return stage[:, 1024:2048]
            if r == 2:
                return xrk_f32[:, 0:1024]
            if r == 3:
                return xrk_f32[:, 1024:2048]
            if r == 4:
                return kab_f32[:, 0:1024]
            if r == 5:
                return zz[1][:, 0:1024]
            if r == 6:
                return zz[1][:, 1024:2048]
            if r == 7:
                return y1[:, 0:1024]
            return y1[:, 1024:2048]

        @block.sync
        def _(sync):
            INC(sync.dma_start(out=bpk[:, :], in_=bpk_d), "sync", s_bld, 16)
            x_dma(sync, "sync", xnew, 0, 0, 4, s_xf, 16)
            x_dma(sync, "sync", xnew, 0, 4, 8, s_xf, 16)
            if NT > 1:
                x_dma(sync, "sync", y2, 1, 0, 4, s_xst1, 16)
                x_dma(sync, "sync", y2, 1, 4, 8, s_xst1, 16)
            for i in range(1, NT):
                W(sync, "sync", s_z, 8 * i)
                x_dma(sync, "sync", xnew, i, 0, 4, s_xf, 16)
                x_dma(sync, "sync", xnew, i, 4, 8, s_xf, 16)

        @block.gpsimd
        def _(g):
            for t, val in ((ones_col[:, :], 1.0), (ones_rowf[:, :], 1.0),
                           (y1[:, :], 0.0), (carry[:, :], 0.0),
                           (consts[:, 0:1], EPS), (consts[:, 1:2], -0.5),
                           (consts[:, 2:3], 0.998), (consts[:, 3:4], 0.001)):
                INC(g.memset(t, val), "pool", s_misc, 1)

            def slab_dma(s):
                w, ke = SLABS[s]
                dst = wslot(s)
                W(g, "pool", s_wcast, s - 8)
                INC(g.dma_start(out=dst[:, 0:512],
                                in_=wsrc[w][:, ke * 1024:ke * 1024 + 512]),
                    "pool", s_wld[s % 9], 16)
                INC(g.dma_start(out=dst[:, 512:1024],
                                in_=wsrc[w][:, ke * 1024 + 512:(ke + 1) * 1024]),
                    "pool", s_wld[s % 9], 16)
            def xs_one(ti, ke):
                n = 8 * ti + ke
                W(g, "pool", s_aevac, 3 * n + 3)
                W(g, "pool", s_scan, n - 7)
                INC(g.tensor_tensor(out=blk(xs, ke), in0=half(sg, ke % 2),
                                    in1=half(tv, ke % 2), op=ALU.mult), "pool", s_attnv, 1)
                INC(g.tensor_scalar(out=blk(aa, ke), in0=half(sd, ke % 2),
                                    scalar1=0.998, scalar2=0.001,
                                    op0=ALU.mult, op1=ALU.add), "pool", s_aav, 1)

            def z0_add(ke):
                W(g, "pool", s_cw, ke + 1)
                INC(g.tensor_tensor(out=blk(Z(0), ke), in0=blk(Z(0), ke),
                                    in1=blk(xnew, ke), op=ALU.add), "pool", s_z, 1)
                if ke % 2 == 1:
                    W(g, "pool", s_z, ke + 1)
                    INC(g.tensor_copy(out=zb1_blk(ke), in_=blk(Z(0), ke)), "pool", s_zb1, 1)
                    INC(g.tensor_tensor(out=zq1_blk(ke), in0=blk(Z(0), ke),
                                        in1=blk(Z(0), ke), op=ALU.mult), "pool", s_zb1, 1)

            slab_dma(0)
            slab_dma(1)
            W(g, "pool", s_xf, 32)
            for s in range(2, NSLAB):
                slab_dma(s)
                if s % 3 == 2 and 5 <= s <= 26:
                    xs_one(0, (s - 5) // 3)
            if NT > 1:
                for ke in range(NB):
                    xs_one(1, ke)
            for ke in range(NB):
                z0_add(ke)
            W(g, "pool", s_wcast, NSLAB - 1)
            x_dma(g, "pool", stage, 2, 0, 4, s_xst, 16)
            x_dma(g, "pool", stage, 2, 4, 8, s_xst, 16)
            W(g, "pool", s_wcast, NSLAB)
            for ke in range(NB):
                INC(g.memset(y1[:, ke * (N + 1):ke * (N + 1) + 1], 0.0),
                    "pool", s_misc, 1)


            for i in range(NT):
                def prefetch(j):
                    if j <= NT - 1:
                        W(g, "pool", s_xbf, j)
                        if j == 2:
                            W(g, "pool", s_wcast, NSLAB)
                        x_dma(g, "pool", stage, j, 0, 4, s_xst, 16)
                        x_dma(g, "pool", stage, j, 4, 8, s_xst, 16)
                if i == 0:
                    prefetch(3)
                else:
                    prefetch(i + 3)
                def z_add(ke):
                    W(g, "pool", s_cw, 8 * (i + 1) + ke + 1)
                    INC(g.tensor_tensor(out=blk(Z(i + 1), ke), in0=blk(Z(i + 1), ke),
                                        in1=blk(xnew, ke), op=ALU.add), "pool", s_z, 1)
                if i + 1 <= NT - 1:
                    W(g, "pool", s_xf, 32 * (i + 2))
                    z_add(0)
                    z_add(1)
                if i + 2 <= NT - 1:
                    xs_one(i + 2, 0)
                    xs_one(i + 2, 1)
                if i + 1 <= NT - 1:
                    z_add(2)
                    z_add(3)
                if i + 2 <= NT - 1:
                    xs_one(i + 2, 2)
                    xs_one(i + 2, 3)
                if i + 1 <= NT - 1:
                    for ke in range(4, NB):
                        z_add(ke)
                if i + 2 <= NT - 1:
                    xs_one(i + 2, 4)
                    xs_one(i + 2, 5)
                W(g, "pool", s_f1mm, 16 * i + 16)
                for ke in range(1, NB, 2):
                    W(g, "pool", s_z2, 8 * i + ke + 1)
                    INC(g.tensor_copy(out=xr_blk(ke), in_=blk(Z(i), ke)), "pool", s_zb2, 1)
                    INC(g.tensor_tensor(out=xk_blk(ke), in0=blk(Z(i), ke),
                                        in1=blk(Z(i), ke), op=ALU.mult), "pool", s_zb2, 1)
                for ke in range(1, NB, 2):
                    if i + 1 <= NT - 1:
                        W(g, "pool", s_ov, 8 * i + ke + 1)
                        W(g, "pool", s_z, 8 * (i + 1) + ke + 1)
                        INC(g.tensor_copy(out=zb1_blk(ke), in_=blk(Z(i + 1), ke)),
                            "pool", s_zb1, 1)
                        INC(g.tensor_tensor(out=zq1_blk(ke), in0=blk(Z(i + 1), ke),
                                            in1=blk(Z(i + 1), ke), op=ALU.mult),
                            "pool", s_zb1, 1)
                if i + 2 <= NT - 1:
                    xs_one(i + 2, 6)
                    xs_one(i + 2, 7)

        @block.scalar
        def _(sc):
            W(sc, "sc", s_bld, 16)
            W(sc, "sc", s_misc, 8)
            W(sc, "sc", s_xf, 32)
            INC(sc.activation(out=xbf[:, 0:2048], in_=xnew[:, :], func=AF.Copy), "sc", s_xbf, 1)
            if NT > 1:
                W(sc, "sc", s_xst1, 32)
                INC(sc.activation(out=xbf[:, 2048:4096], in_=y2[:, :], func=AF.Copy),
                    "sc", s_xbf, 1)

            def y2_store(i):
                W(sc, "sc", s_y2, 8 * i + 8)
                srcv = y2[:, :].rearrange("p (kd n) -> p kd n", kd=NB)
                INC(sc.dma_start(out=yTr[:, 0:4, i * N:(i + 1) * N], in_=srcv[:, 0:4, :]),
                    "sc", s_yst, 16)
                INC(sc.dma_start(out=yTr[:, 4:8, i * N:(i + 1) * N], in_=srcv[:, 4:8, :]),
                    "sc", s_yst, 16)

            def trio_evac(ti, ke):
                n = 8 * ti + ke
                W(sc, "sc", s_attnv, n - 1)
                W(sc, "sc", s_aav, n - 1)
                par = ke % 2
                W(sc, "sc", s_amm, 3 * n + 1)
                INC(sc.activation(out=half(sg, par), in_=pg[:, 0:N], func=AF.Sigmoid,
                                  bias=bpk[:, ke:ke + 1]), "sc", s_aevac, 1)
                W(sc, "sc", s_amm, 3 * n + 2)
                INC(sc.activation(out=half(tv, par), in_=pv[:, 0:N], func=AF.Tanh,
                                  bias=bpk[:, 8 + ke:9 + ke]), "sc", s_aevac, 1)
                W(sc, "sc", s_amm, 3 * n + 3)
                INC(sc.activation(out=half(sd, par), in_=pd[:, 0:N], func=AF.Sigmoid,
                                  bias=bpk[:, 16 + ke:17 + ke]), "sc", s_aevac, 1)

            for ke in range(NB):
                trio_evac(0, ke)
            if NT > 1:
                for ke in range(NB):
                    trio_evac(1, ke)
            for ke in range(0, NB, 2):
                W(sc, "sc", s_z, ke + 1)
                INC(sc.activation(out=zb1_blk(ke), in_=blk(Z(0), ke), func=AF.Copy),
                    "sc", s_zb1, 1)
                INC(sc.activation(out=zq1_blk(ke), in_=blk(Z(0), ke), func=AF.Square),
                    "sc", s_zb1, 1)

            for i in range(NT):
                q1, q2 = 2 * i + 1, 2 * i + 2
                if i == 0 and NT > 2:
                    W(sc, "sc", s_xst, 32)
                    W(sc, "sc", s_amm, 24)
                    INC(sc.activation(out=xbf[:, 0:2048], in_=stage[:, :],
                                      func=AF.Copy), "sc", s_xbf, 1)
                W(sc, "sc", s_rv, q1)
                iln = sc.activation(out=row_t[:, :], in_=row_var[:, :], func=AF.Ln,
                                    bias=consts[0:1, 0:1])
                sfence(sc, iln)
                INC(sc.activation(out=row_inv[:, :], in_=row_t[:, :], func=AF.Exp,
                                  scale=consts[0:1, 1:2]), "sc", s_rs, 1)
                if i + 2 <= NT - 1:
                    for ke in range(6):
                        trio_evac(i + 2, ke)
                W(sc, "sc", s_f2mm, 8 * i)
                for ke in range(NB):
                    W(sc, "sc", s_ov, 8 * (i - 1) + ke + 1)
                    W(sc, "sc", s_xrk, 16 * i + 9 + ke)
                    W(sc, "sc", s_f1mm, 16 * i + ke + 1)
                    INC(sc.activation(out=blk(rbuf, ke), in_=f1b(ke)[:, 0:N],
                                      func=AF.Sigmoid), "sc", s_f1ev, 1)
                for ke in range(NB):
                    par = ke % 2
                    W(sc, "sc", s_f1mm, 16 * i + 8 + ke + 1)
                    INC(sc.activation(out=half(rk, par), in_=f1b(ke)[:, 0:N],
                                      func=AF.Relu), "sc", s_f1ev, 1)
                    W(sc, "sc", s_f1ev, 24 * i + 8 + 2 * ke + 1)
                    INC(sc.activation(out=blk(kab, ke), in_=half(rk, par), func=AF.Square),
                        "sc", s_f1ev, 1)
                if i >= 1:
                    y2_store(i - 1)
                W(sc, "sc", s_f1mm, 16 * i + 16)
                for ke in range(0, NB, 2):
                    W(sc, "sc", s_z2, 8 * i + ke + 1)
                    INC(sc.activation(out=xr_blk(ke), in_=blk(Z(i), ke), func=AF.Copy),
                        "sc", s_zb2, 1)
                    INC(sc.activation(out=xk_blk(ke), in_=blk(Z(i), ke), func=AF.Square),
                        "sc", s_zb2, 1)
                for ke in range(0, NB, 2):
                    if i + 1 <= NT - 1:
                        W(sc, "sc", s_ov, 8 * i + ke + 1)
                        W(sc, "sc", s_z, 8 * (i + 1) + ke + 1)
                        INC(sc.activation(out=zb1_blk(ke), in_=blk(Z(i + 1), ke),
                                          func=AF.Copy), "sc", s_zb1, 1)
                        INC(sc.activation(out=zq1_blk(ke), in_=blk(Z(i + 1), ke),
                                          func=AF.Square), "sc", s_zb1, 1)
                if i + 2 <= NT - 1:
                    trio_evac(i + 2, 6)
                W(sc, "sc", s_rv, q2)
                iln2 = sc.activation(out=row_t[:, :], in_=row_var[:, :], func=AF.Ln,
                                     bias=consts[0:1, 0:1])
                sfence(sc, iln2)
                INC(sc.activation(out=row_inv[:, :], in_=row_t[:, :], func=AF.Exp,
                                  scale=consts[0:1, 1:2]), "sc", s_rs, 1)
                W(sc, "sc", s_bcast, q2)
                INC(sc.activation(out=dbuf[:, :], in_=psb[:, 0:2 * N], func=AF.Copy),
                    "sc", s_pcp, 1)
                if i + 3 <= NT - 1:
                    j = i + 3
                    W(sc, "sc", s_xst, 32 * (j - 1))
                    W(sc, "sc", s_amm, 24 * (j - 1))
                    INC(sc.activation(out=xbf[:, (j % 2) * 2048:(j % 2) * 2048 + 2048],
                                      in_=stage[:, :], func=AF.Copy), "sc", s_xbf, 1)
                if i + 2 <= NT - 1:
                    trio_evac(i + 2, 7)
            y2_store(NT - 1)
            W(sc, "sc", s_yst, 32 * NT)

        @block.vector
        def _(v):
            W(v, "vec", s_misc, 8)
            W(v, "vec", s_bld, 16)

            def scan_one(ti, ke):
                n = 8 * ti + ke
                W(v, "vec", s_attnv, n + 1)
                W(v, "vec", s_aav, n + 1)
                W(v, "vec", s_cw, n - 7)
                if ti >= 2:
                    W(v, "vec", s_y2, 8 * (ti - 1) - (4 if ke <= 3 else 0))
                INC(v.tensor_tensor_scan(out=blk(Z(ti), ke), data0=blk(aa, ke),
                                           data1=blk(xs, ke),
                                           initial=carry[:, ke:ke + 1],
                                           op0=ALU.mult, op1=ALU.add), "vec", s_scan, 1)
                W(v, "vec", s_scan, n + 1)
                INC(v.tensor_copy(out=carry[:, ke:ke + 1],
                                  in_=bass.AP(Z(ti), ke * N + N - 1, [[2048, 128], [1, 1]])),
                    "vec", s_cw, 1)

            def wb_dst(s):
                w, ke = SLABS[s]
                return wb[w][:, ke * 1024:(ke + 1) * 1024]
            for s in range(NSLAB):
                W(v, "vec", s_wld[s % 9], 32 * (s // 9 + 1))
                INC(v.tensor_copy(out=wb_dst(s), in_=wslot(s)), "vec", s_wcast, 1)
                if s % 3 == 2 and 8 <= s <= 29:
                    scan_one(0, (s - 8) // 3)

            def apply2a(j):
                W(v, "vec", s_pcp, j + 1)
                if j == 0 and NT > 1:
                    W(v, "vec", s_xbf, 2)
                W(v, "vec", s_yst, 32 * j)
                ip2 = v.tensor_tensor(out=z3h(j, 0), in0=z3h(j, 0), in1=Mb_d4,
                                      op=ALU.subtract)
                vfence(v, ip2)
                INC(v.tensor_tensor(out=y2_3a, in0=z3h(j, 0), in1=Sb_d4, op=ALU.mult),
                    "vec", s_y2, 4)

            def apply2b(j):
                ip2 = v.tensor_tensor(out=z3h(j, 1), in0=z3h(j, 1), in1=Mb_d4,
                                      op=ALU.subtract)
                vfence(v, ip2)
                INC(v.tensor_tensor(out=y2_3b, in0=z3h(j, 1), in1=Sb_d4, op=ALU.mult),
                    "vec", s_y2, 4)

            def rows(q):
                W(v, "vec", s_stats, q)
                INC(v.tensor_scalar(out=row_mq[:, :], in0=psb[0:1, 0:2 * N],
                                    scalar1=1.0 / D, scalar2=None, op0=ALU.mult),
                    "vec", s_rpsum, 1)
                W(v, "vec", s_rpsum, q)
                i3 = v.tensor_tensor(out=row_t[:, :], in0=row_mq[0:1, 0:N],
                                     in1=row_mq[0:1, 0:N], op=ALU.mult)
                vfence(v, i3)
                INC(v.tensor_tensor(out=row_var[:, :], in0=row_mq[0:1, N:2 * N],
                                    in1=row_t[:, :], op=ALU.subtract), "vec", s_rv, 1)

            for i in range(NT):
                q1, q2 = 2 * i + 1, 2 * i + 2
                if i >= 1:
                    apply2a(i - 1)
                rows(q1)
                if i + 1 <= NT - 1:
                    scan_one(i + 1, 0)
                    scan_one(i + 1, 1)
                if i >= 1:
                    iyc = v.tensor_copy(out=y1_c0, in_=y1_cN)
                    ordv_n[0] += 1
                    INC(iyc, "vec", s_ordv, 1)
                W(v, "vec", s_bcast, q1)
                ip1 = v.tensor_tensor(out=z3(i), in0=z3(i), in1=Mb_v, op=ALU.subtract)
                vfence(v, ip1)
                if i + 1 <= NT - 1:
                    scan_one(i + 1, 2)
                INC(v.tensor_tensor(out=y1_margin, in0=z3(i), in1=Sb_v, op=ALU.mult),
                    "vec", s_pass, 1)
                W(v, "vec", s_pass, i + 1)
                if i + 1 <= NT - 1:
                    scan_one(i + 1, 3)
                if i == 0:
                    W(v, "vec", s_misc, 16)
                idt = v.tensor_tensor(out=rbuf_blks, in0=y1v_full, in1=xpv_full,
                                      op=ALU.subtract)
                vfence(v, idt)
                for ke in range(NB):
                    INC(v.scalar_tensor_tensor(out=xr_blk(ke), in0=blk(rbuf, ke),
                                               scalar=bpk[:, 24 + ke:25 + ke], in1=xpv(ke),
                                               op0=ALU.mult, op1=ALU.add), "vec", s_xrk, 1)
                if i >= 1:
                    apply2b(i - 1)
                for ke in range(NB):
                    INC(v.scalar_tensor_tensor(out=xk_blk(ke), in0=blk(rbuf, ke),
                                               scalar=bpk[:, 32 + ke:33 + ke], in1=xpv(ke),
                                               op0=ALU.mult, op1=ALU.add), "vec", s_xrk, 1)
                if i + 1 <= NT - 1:
                    for ke in range(4, NB):
                        scan_one(i + 1, ke)
                for ke in range(NB):
                    W(v, "vec", s_f2mm, 8 * i + ke + 1)
                    INC(v.tensor_tensor(out=blk(Z(i), ke), in0=blk(rbuf, ke),
                                        in1=po[ke % 2][:, 0:N], op=ALU.mult), "vec", s_ov, 1)
                    W(v, "vec", s_ov, 8 * i + ke + 1)
                    INC(v.tensor_tensor(out=blk(Z(i), ke), in0=blk(Z(i), ke), in1=y1v(ke),
                                        op=ALU.add), "vec", s_z2, 1)
                rows(q2)
            apply2a(NT - 1)
            apply2b(NT - 1)

        @block.tensor
        def _(t):
            W(t, "ten", s_misc, 8)

            def trio(ti, ke):
                n = 8 * ti + ke
                W(t, "ten", s_wcast, 3 * ke + 3)
                W(t, "ten", s_xbf, ti + 1)
                for gi, (w, bank) in enumerate(((0, pg), (1, pv), (2, pd))):
                    W(t, "ten", s_aevac, 3 * n - 2 + gi)
                    for kd in range(NB):
                        ins = t.matmul(out=bank[:, 0:N], lhsT=wtile(w, ke, kd),
                                       rhs=xbf_blk(ti, kd), start=(kd == 0), stop=(kd == NB - 1))
                        if kd == NB - 1:
                            INC(ins, "ten", s_amm, 1)

            def stats(i, ln, extra):
                q = 2 * i + ln
                W(t, "ten", s_zb1 if ln == 1 else s_zb2, 16 * (i + 1))
                W(t, "ten", s_rpsum, q - 1)
                extra()
                psrc = zz1_pair if ln == 1 else xrk_pair
                for kc in range(NB):
                    ins = t.matmul(out=psb[0:1, 0:2 * N], lhsT=ones_col[:, :],
                                   rhs=psrc(kc), start=(kc == 0), stop=(kc == NB - 1))
                    if kc == NB - 1:
                        INC(ins, "ten", s_stats, 1)

            def bcast(q):
                W(t, "ten", s_rs, q)
                t.matmul(out=psb[:, 0:N], lhsT=ones_rowf[:, :], rhs=row_mq[0:1, 0:N],
                         start=True, stop=True)
                INC(t.matmul(out=psb[:, N:2 * N], lhsT=ones_rowf[:, :], rhs=row_inv[:, :],
                             start=True, stop=True), "ten", s_bcast, 1)

            for ke in range(NB):
                trio(0, ke)
            if NT > 1:
                for ke in range(NB):
                    trio(1, ke)
            for i in range(NT):
                q1, q2 = 2 * i + 1, 2 * i + 2
                has2 = i + 2 <= NT - 1
                stats(i, 1, lambda: W(t, "ten", s_pcp, i))
                if has2:
                    trio(i + 2, 0)
                for _ in range(8):
                    t.matmul(out=pkk[0:1, 0:N], lhsT=ones_col[:, :],
                             rhs=xbf[:, 0:N], start=True, stop=True)
                bcast(q1)
                if has2:
                    for ke in range(1, 6):
                        trio(i + 2, ke)
                W(t, "ten", s_xrk, 16 * i + 8)
                for ke in range(NB):
                    W(t, "ten", s_wcast, 24 + 2 * ke + 1)
                    gate = (24 * i + ke - 1) if ke >= 2 else (24 * i - 3 + 2 * ke)
                    W(t, "ten", s_f1ev, gate)
                    for kc in range(NB):
                        ins = t.matmul(out=f1b(ke)[:, 0:N], lhsT=wtile(3, ke, kc),
                                       rhs=xr_blk(kc), start=(kc == 0), stop=(kc == NB - 1))
                        if kc == NB - 1:
                            INC(ins, "ten", s_f1mm, 1)
                W(t, "ten", s_xrk, 16 * i + 16)
                for ke in range(NB):
                    W(t, "ten", s_wcast, 24 + 2 * ke + 2)
                    gate = (24 * i + 2 * ke + 5) if ke >= 2 else (24 * i + 7 + ke)
                    W(t, "ten", s_f1ev, gate)
                    for kc in range(NB):
                        ins = t.matmul(out=f1b(ke)[:, 0:N], lhsT=wtile(4, ke, kc),
                                       rhs=xk_blk(kc), start=(kc == 0), stop=(kc == NB - 1))
                        if kc == NB - 1:
                            INC(ins, "ten", s_f1mm, 1)
                W(t, "ten", s_f1ev, 24 * i + 24)
                for ke in range(NB):
                    n = 8 * i + ke
                    W(t, "ten", s_wcast, 40 + ke + 1)
                    W(t, "ten", s_ov, n - 1)
                    for kc in range(NB):
                        ins = t.matmul(out=po[ke % 2][:, 0:N], lhsT=wtile(5, ke, kc),
                                       rhs=blk(kab, kc), start=(kc == 0), stop=(kc == NB - 1))
                        if kc == NB - 1:
                            INC(ins, "ten", s_f2mm, 1)
                if has2:
                    trio(i + 2, 6)
                stats(i, 2, lambda: W(t, "ten", s_pass, i + 1))
                if has2:
                    trio(i + 2, 7)
                for _ in range(20):
                    t.matmul(out=pkr[0:1, 0:N], lhsT=ones_col[:, :],
                             rhs=xbf[:, 0:N], start=True, stop=True)
                bcast(q2)
    return nc


def slab_lin(w):
    return np.ascontiguousarray(
        np.asarray(w, np.float32).reshape(8, 128, 8, 128)
        .transpose(1, 2, 0, 3).reshape(128, 8192))


def kernel(L0_attn_x, L0_attn_Wg, L0_attn_bg, L0_attn_Wv, L0_attn_bv, L0_attn_Wd,
           L0_attn_bd, L0_ffn_W_r, L0_ffn_W_k, L0_ffn_W_v, L0_ffn_mu_r, L0_ffn_mu_k):
    x = np.asarray(L0_attn_x, dtype=np.float32)
    B = x.shape[0]
    packs = [np.asarray(a, np.float32).reshape(NB, 128).T
             for a in (L0_attn_bg, L0_attn_bv, L0_attn_bd, L0_ffn_mu_r, L0_ffn_mu_k)]
    bpk = np.ascontiguousarray(np.concatenate(packs, axis=1))
    shared = {
        "wg": slab_lin(L0_attn_Wg),
        "wv": slab_lin(L0_attn_Wv),
        "wd": slab_lin(L0_attn_Wd),
        "wr": slab_lin(np.asarray(L0_ffn_W_r, np.float32).T),
        "wk": slab_lin(np.asarray(L0_ffn_W_k, np.float32).T),
        "wvf": slab_lin(np.asarray(L0_ffn_W_v, np.float32).T),
        "bpk": bpk,
    }
    nc = build_nc()
    in_maps = []
    for b in range(B):
        m = dict(shared)
        m["xT"] = np.ascontiguousarray(x[b].T)
        in_maps.append(m)
    res = run_bass_kernel_spmd(nc, in_maps, core_ids=list(range(B)))
    out = np.empty((B, SEQ, D), dtype=np.float32)
    for b in range(B):
        out[b] = res.results[b]["yT"].T
    return out


# version 51
# speedup vs baseline: 1.0097x; 1.0097x over previous
import contextlib
import numpy as np
import concourse.bass as bass
import concourse.mybir as mybir
from concourse.bass_utils import run_bass_kernel_spmd

F32 = mybir.dt.float32
BF16 = mybir.dt.bfloat16
AF = mybir.ActivationFunctionType
ALU = mybir.AluOpType

D = 1024
NB = 8
NT = 16
N = 256
SEQ = NT * N
EPS = 1e-5

SLABS = [(w, ke) for ke in range(NB) for w in (0, 1, 2)] \
      + [(w, ke) for ke in range(NB) for w in (3, 4)] \
      + [(5, ke) for ke in range(NB)]
NSLAB = len(SLABS)


def build_nc(shadow=None):
    nc = bass.Bass("TRN2", target_bir_lowering=False)

    xT = nc.dram_tensor("xT", [D, SEQ], F32, kind="ExternalInput").ap()
    wnames = ["wg", "wv", "wd", "wr", "wk", "wvf"]
    wsrc = [nc.dram_tensor(n, [128, NB * 1024], F32, kind="ExternalInput").ap()
            for n in wnames]
    bpk_d = nc.dram_tensor("bpk", [128, 40], F32, kind="ExternalInput").ap()
    yT = nc.dram_tensor("yT", [D, SEQ], F32, kind="ExternalOutput").ap()
    xTr = xT.rearrange("(kd p) t -> p kd t", p=128)
    yTr = yT.rearrange("(kd p) t -> p kd t", p=128)

    es = contextlib.ExitStack()
    with es:
        def sb(name, shape, dt):
            return es.enter_context(nc.sbuf_tensor(name, shape, dt))
        wb = [sb(f"wb{n}", [128, NB * 1024], BF16) for n in wnames]
        stage = sb("stage", [128, 2048], F32)
        xbf = sb("xbf", [128, 2 * 2048], BF16)
        sg = sb("sg", [128, 512], F32)
        tv = sb("tv", [128, 512], F32)
        sd = sb("sd", [128, 512], F32)
        xs = sb("xs", [128, 2048], F32)
        aa = sb("aa", [128, 2048], F32)
        zz = [sb("z0", [128, 2048], F32), sb("z1", [128, 2048], F32)]
        carry = sb("carry", [128, NB], F32)
        xnew = sb("xnew", [128, 2048], F32)
        y1 = sb("y1", [128, NB * (N + 1)], F32)
        dbuf = sb("dbuf", [128, 512], F32)
        xrk = sb("xrk", [128, 4096], BF16)
        rbuf = sb("rbuf", [128, 2048], F32)
        rk = sb("rk", [128, 512], F32)
        kab = sb("kab", [128, 2048], BF16)
        y2 = sb("y2", [128, 2048], F32)
        bpk = sb("bpk_s", [128, 40], F32)
        ones_col = sb("ones_col", [128, 1], BF16)
        ones_rowf = sb("ones_rowf", [1, 128], F32)
        row_mq = sb("row_mq", [1, 2 * N], F32)
        row_t = sb("row_t", [1, N], F32)
        row_var = sb("row_var", [1, N], F32)
        row_inv = sb("row_inv", [1, N], F32)
        consts = sb("consts", [128, 4], F32)

        def ps(name):
            return es.enter_context(nc.psum_tensor(name, [128, 512], F32))
        pg, pv, pd = ps("pg"), ps("pv"), ps("pd")
        pkr, pkk = ps("pkr"), ps("pkk")
        def f1b(ke):
            return pkr if ke % 2 == 0 else pkk
        po = [ps("po0"), ps("po1")]
        psb = ps("psb")

        def sem(name):
            return es.enter_context(nc.semaphore(name))
        s_bld, s_xf, s_xst, s_yst = sem("bld"), sem("xf"), sem("xst"), sem("yst")
        s_xst1 = sem("xst1")
        s_wld = [sem(f"wld{k}") for k in range(9)]
        s_misc, s_wcast, s_xbf = sem("misc"), sem("wcast"), sem("xbf")
        s_amm, s_aevac, s_attnv, s_scan = sem("amm"), sem("aevac"), sem("attnv"), sem("scan")
        s_aav = sem("aav")
        s_z, s_zb1, s_zb2, s_stats = sem("z"), sem("zb1"), sem("zb2"), sem("stats")
        s_rv, s_rs, s_rpsum = sem("rv"), sem("rs"), sem("rpsum")
        s_bcast, s_pass, s_d, s_xrk = sem("bcast"), sem("pass"), sem("d"), sem("xrk")
        s_f1mm, s_f1ev, s_f2mm, s_ov = sem("f1mm"), sem("f1ev"), sem("f2mm"), sem("ov")
        s_z2, s_y2 = sem("z2"), sem("y2")
        s_ordv, s_ords, s_ordg = sem("ordv"), sem("ords"), sem("ordg")
        s_cw = sem("cw")
        s_pcp = sem("pcp")
        ordv_n = [0]
        ords_n = [0]
        ordg_n = [0]

        block = es.enter_context(nc.Block())

        def ev(engine, kind, s, v):
            if shadow is not None:
                shadow[engine].append((kind, s.name, v))

        def W(eng, name, s, v):
            if v <= 0:
                return
            eng.wait_ge(s, v)
            ev(name, "wait", s, v)

        def INC(instr, name, s, v):
            instr.then_inc(s, v)
            ev(name, "inc", s, v)
            return instr

        def vfence(v, ins):
            ordv_n[0] += 1
            INC(ins, "vec", s_ordv, 1)
            W(v, "vec", s_ordv, ordv_n[0])

        def gfence(g, ins):
            ordg_n[0] += 1
            INC(ins, "pool", s_ordg, 1)
            W(g, "pool", s_ordg, ordg_n[0])

        def sfence(sc, ins):
            ords_n[0] += 1
            INC(ins, "sc", s_ords, 1)
            W(sc, "sc", s_ords, ords_n[0])

        def half(t, par):
            return t[:, par * N:(par + 1) * N]
        def blk(t, k):
            return t[:, k * N:(k + 1) * N]
        def Z(i):
            return zz[i % 2]
        def z3(i):
            return bass.AP(Z(i), 0, [[2048, 128], [N, NB], [1, N]])
        def wtile(w, ke, kd):
            return wb[w][:, (ke * 8 + kd) * 128:(ke * 8 + kd) * 128 + 128]
        def xbf_blk(j, kd):
            off = (j % 2) * 2048 + kd * N
            return xbf[:, off:off + N]
        y2_3a = bass.AP(y2, 0, [[2048, 128], [N, 4], [1, N]])
        y2_3b = bass.AP(y2, 4 * N, [[2048, 128], [N, 4], [1, N]])
        def z3h(j, h):
            return bass.AP(Z(j), h * 4 * N, [[2048, 128], [N, 4], [1, N]])
        y1w = N + 1
        y1_margin = bass.AP(y1, 1, [[NB * y1w, 128], [y1w, NB], [1, N]])
        def y1v(ke):
            return bass.AP(y1, ke * y1w + 1, [[NB * y1w, 128], [1, N]])
        def xpv(ke):
            return bass.AP(y1, ke * y1w, [[NB * y1w, 128], [1, N]])
        y1_c0 = bass.AP(y1, 0, [[NB * y1w, 128], [y1w, NB]])
        y1_cN = bass.AP(y1, N, [[NB * y1w, 128], [y1w, NB]])
        Mb_v = bass.AP(psb, 0, [[512, 128], [0, NB], [1, N]])
        Sb_v = bass.AP(psb, N, [[512, 128], [0, NB], [1, N]])
        Mb_d4 = bass.AP(dbuf, 0, [[512, 128], [0, 4], [1, N]])
        Sb_d4 = bass.AP(dbuf, N, [[512, 128], [0, 4], [1, N]])
        p_sum = psb[0:1, 0:N]
        p_sq = psb[0:1, N:2 * N]
        def xr_blk(k):
            return xrk[:, k * 512:k * 512 + 256]
        def xk_blk(k):
            return xrk[:, k * 512 + 256:(k + 1) * 512]
        def xrk_pair(k):
            return xrk[:, k * 512:(k + 1) * 512]
        def zb1_blk(k):
            return bass.AP(rbuf, k * 256, [[2048, 128], [1, 128]]).bitcast(BF16)
        def zq1_blk(k):
            return bass.AP(rbuf, k * 256 + 128, [[2048, 128], [1, 128]]).bitcast(BF16)
        def zz1_pair(k):
            return bass.AP(rbuf, k * 256, [[2048, 128], [1, 256]]).bitcast(BF16)
        y1v_full = bass.AP(y1, 1, [[NB * y1w, 128], [y1w, NB], [1, N]])
        xpv_full = bass.AP(y1, 0, [[NB * y1w, 128], [y1w, NB], [1, N]])
        rbuf_blks = bass.AP(rbuf, 0, [[2048, 128], [N, NB], [1, N]])

        def x_dma(q, qname, dst_sb, ti, lo, hi, s, n16):
            dst = dst_sb[:, :].rearrange("p (kd n) -> p kd n", kd=NB)[:, lo:hi, :]
            src = xTr[:, lo:hi, ti * N:(ti + 1) * N]
            INC(q.dma_start(out=dst, in_=src), qname, s, n16)

        xrk_f32 = xrk[:, :].bitcast(F32)
        kab_f32 = kab[:, :].bitcast(F32)

        def wslot(s):
            r = s % 9
            if r == 0:
                return stage[:, 0:1024]
            if r == 1:
                return stage[:, 1024:2048]
            if r == 2:
                return xrk_f32[:, 0:1024]
            if r == 3:
                return xrk_f32[:, 1024:2048]
            if r == 4:
                return kab_f32[:, 0:1024]
            if r == 5:
                return zz[1][:, 0:1024]
            if r == 6:
                return zz[1][:, 1024:2048]
            if r == 7:
                return y1[:, 0:1024]
            return y1[:, 1024:2048]

        @block.sync
        def _(sync):
            INC(sync.dma_start(out=bpk[:, :], in_=bpk_d), "sync", s_bld, 16)
            x_dma(sync, "sync", xnew, 0, 0, 4, s_xf, 16)
            x_dma(sync, "sync", xnew, 0, 4, 8, s_xf, 16)
            if NT > 1:
                x_dma(sync, "sync", y2, 1, 0, 4, s_xst1, 16)
                x_dma(sync, "sync", y2, 1, 4, 8, s_xst1, 16)
            for i in range(1, NT):
                W(sync, "sync", s_z, 8 * i)
                x_dma(sync, "sync", xnew, i, 0, 4, s_xf, 16)
                x_dma(sync, "sync", xnew, i, 4, 8, s_xf, 16)

        @block.gpsimd
        def _(g):
            for t, val in ((ones_col[:, :], 1.0), (ones_rowf[:, :], 1.0),
                           (y1[:, :], 0.0), (carry[:, :], 0.0),
                           (consts[:, 0:1], EPS), (consts[:, 1:2], -0.5),
                           (consts[:, 2:3], 0.998), (consts[:, 3:4], 0.001)):
                INC(g.memset(t, val), "pool", s_misc, 1)

            def slab_dma(s):
                w, ke = SLABS[s]
                dst = wslot(s)
                W(g, "pool", s_wcast, s - 8)
                INC(g.dma_start(out=dst[:, 0:512],
                                in_=wsrc[w][:, ke * 1024:ke * 1024 + 512]),
                    "pool", s_wld[s % 9], 16)
                INC(g.dma_start(out=dst[:, 512:1024],
                                in_=wsrc[w][:, ke * 1024 + 512:(ke + 1) * 1024]),
                    "pool", s_wld[s % 9], 16)
            def xs_one(ti, ke):
                n = 8 * ti + ke
                W(g, "pool", s_aevac, 3 * n + 3)
                W(g, "pool", s_scan, n - 7)
                INC(g.tensor_tensor(out=blk(xs, ke), in0=half(sg, ke % 2),
                                    in1=half(tv, ke % 2), op=ALU.mult), "pool", s_attnv, 1)
                INC(g.tensor_scalar(out=blk(aa, ke), in0=half(sd, ke % 2),
                                    scalar1=0.998, scalar2=0.001,
                                    op0=ALU.mult, op1=ALU.add), "pool", s_aav, 1)

            def z0_add(ke):
                W(g, "pool", s_cw, ke + 1)
                INC(g.tensor_tensor(out=blk(Z(0), ke), in0=blk(Z(0), ke),
                                    in1=blk(xnew, ke), op=ALU.add), "pool", s_z, 1)
                if ke % 2 == 1:
                    W(g, "pool", s_z, ke + 1)
                    INC(g.tensor_copy(out=zb1_blk(ke), in_=blk(Z(0), ke)), "pool", s_zb1, 1)
                    INC(g.tensor_tensor(out=zq1_blk(ke), in0=blk(Z(0), ke),
                                        in1=blk(Z(0), ke), op=ALU.mult), "pool", s_zb1, 1)

            slab_dma(0)
            slab_dma(1)
            W(g, "pool", s_xf, 32)
            for s in range(2, NSLAB):
                slab_dma(s)
                if s % 3 == 2 and 5 <= s <= 26:
                    xs_one(0, (s - 5) // 3)
            if NT > 1:
                for ke in range(NB):
                    xs_one(1, ke)
            for ke in range(NB):
                z0_add(ke)
            W(g, "pool", s_wcast, NSLAB - 1)
            x_dma(g, "pool", stage, 2, 0, 4, s_xst, 16)
            x_dma(g, "pool", stage, 2, 4, 8, s_xst, 16)
            W(g, "pool", s_wcast, NSLAB)
            for ke in range(NB):
                INC(g.memset(y1[:, ke * (N + 1):ke * (N + 1) + 1], 0.0),
                    "pool", s_misc, 1)


            for i in range(NT):
                def prefetch(j):
                    if j <= NT - 1:
                        W(g, "pool", s_xbf, j)
                        if j == 2:
                            W(g, "pool", s_wcast, NSLAB)
                        x_dma(g, "pool", stage, j, 0, 4, s_xst, 16)
                        x_dma(g, "pool", stage, j, 4, 8, s_xst, 16)
                if i == 0:
                    prefetch(3)
                else:
                    prefetch(i + 3)
                def z_add(ke):
                    W(g, "pool", s_cw, 8 * (i + 1) + ke + 1)
                    INC(g.tensor_tensor(out=blk(Z(i + 1), ke), in0=blk(Z(i + 1), ke),
                                        in1=blk(xnew, ke), op=ALU.add), "pool", s_z, 1)
                if i + 1 <= NT - 1:
                    W(g, "pool", s_xf, 32 * (i + 2))
                    z_add(0)
                    z_add(1)
                if i + 2 <= NT - 1:
                    xs_one(i + 2, 0)
                    xs_one(i + 2, 1)
                if i + 1 <= NT - 1:
                    z_add(2)
                    z_add(3)
                if i + 2 <= NT - 1:
                    xs_one(i + 2, 2)
                    xs_one(i + 2, 3)
                if i + 1 <= NT - 1:
                    for ke in range(4, NB):
                        z_add(ke)
                if i + 2 <= NT - 1:
                    xs_one(i + 2, 4)
                    xs_one(i + 2, 5)
                W(g, "pool", s_f1mm, 16 * i + 16)
                for ke in range(1, NB, 2):
                    W(g, "pool", s_z2, 8 * i + ke + 1)
                    INC(g.tensor_copy(out=xr_blk(ke), in_=blk(Z(i), ke)), "pool", s_zb2, 1)
                    INC(g.tensor_tensor(out=xk_blk(ke), in0=blk(Z(i), ke),
                                        in1=blk(Z(i), ke), op=ALU.mult), "pool", s_zb2, 1)
                for ke in range(1, NB, 2):
                    if i + 1 <= NT - 1:
                        W(g, "pool", s_ov, 8 * i + ke + 1)
                        W(g, "pool", s_z, 8 * (i + 1) + ke + 1)
                        INC(g.tensor_copy(out=zb1_blk(ke), in_=blk(Z(i + 1), ke)),
                            "pool", s_zb1, 1)
                        INC(g.tensor_tensor(out=zq1_blk(ke), in0=blk(Z(i + 1), ke),
                                            in1=blk(Z(i + 1), ke), op=ALU.mult),
                            "pool", s_zb1, 1)
                if i + 2 <= NT - 1:
                    xs_one(i + 2, 6)
                    xs_one(i + 2, 7)

        @block.scalar
        def _(sc):
            W(sc, "sc", s_bld, 16)
            W(sc, "sc", s_misc, 8)
            W(sc, "sc", s_xf, 32)
            INC(sc.activation(out=xbf[:, 0:2048], in_=xnew[:, :], func=AF.Copy), "sc", s_xbf, 1)
            if NT > 1:
                W(sc, "sc", s_xst1, 32)
                INC(sc.activation(out=xbf[:, 2048:4096], in_=y2[:, :], func=AF.Copy),
                    "sc", s_xbf, 1)

            def y2_store(i):
                W(sc, "sc", s_y2, 8 * i + 8)
                srcv = y2[:, :].rearrange("p (kd n) -> p kd n", kd=NB)
                INC(sc.dma_start(out=yTr[:, 0:4, i * N:(i + 1) * N], in_=srcv[:, 0:4, :]),
                    "sc", s_yst, 16)
                INC(sc.dma_start(out=yTr[:, 4:8, i * N:(i + 1) * N], in_=srcv[:, 4:8, :]),
                    "sc", s_yst, 16)

            def trio_evac(ti, ke):
                n = 8 * ti + ke
                W(sc, "sc", s_attnv, n - 1)
                W(sc, "sc", s_aav, n - 1)
                par = ke % 2
                W(sc, "sc", s_amm, 3 * n + 1)
                INC(sc.activation(out=half(sg, par), in_=pg[:, 0:N], func=AF.Sigmoid,
                                  bias=bpk[:, ke:ke + 1]), "sc", s_aevac, 1)
                W(sc, "sc", s_amm, 3 * n + 2)
                INC(sc.activation(out=half(tv, par), in_=pv[:, 0:N], func=AF.Tanh,
                                  bias=bpk[:, 8 + ke:9 + ke]), "sc", s_aevac, 1)
                W(sc, "sc", s_amm, 3 * n + 3)
                INC(sc.activation(out=half(sd, par), in_=pd[:, 0:N], func=AF.Sigmoid,
                                  bias=bpk[:, 16 + ke:17 + ke]), "sc", s_aevac, 1)

            for ke in range(NB):
                trio_evac(0, ke)
            if NT > 1:
                for ke in range(NB):
                    trio_evac(1, ke)
            for ke in range(0, NB, 2):
                W(sc, "sc", s_z, ke + 1)
                INC(sc.activation(out=zb1_blk(ke), in_=blk(Z(0), ke), func=AF.Copy),
                    "sc", s_zb1, 1)
                INC(sc.activation(out=zq1_blk(ke), in_=blk(Z(0), ke), func=AF.Square),
                    "sc", s_zb1, 1)

            for i in range(NT):
                q1, q2 = 2 * i + 1, 2 * i + 2
                if i == 0 and NT > 2:
                    W(sc, "sc", s_xst, 32)
                    W(sc, "sc", s_amm, 24)
                    INC(sc.activation(out=xbf[:, 0:2048], in_=stage[:, :],
                                      func=AF.Copy), "sc", s_xbf, 1)
                W(sc, "sc", s_rv, q1)
                iln = sc.activation(out=row_t[:, :], in_=row_var[:, :], func=AF.Ln,
                                    bias=consts[0:1, 0:1])
                sfence(sc, iln)
                INC(sc.activation(out=row_inv[:, :], in_=row_t[:, :], func=AF.Exp,
                                  scale=consts[0:1, 1:2]), "sc", s_rs, 1)
                if i + 2 <= NT - 1:
                    for ke in range(6):
                        trio_evac(i + 2, ke)
                W(sc, "sc", s_f2mm, 8 * i)
                for ke in range(NB):
                    W(sc, "sc", s_ov, 8 * (i - 1) + ke + 1)
                    W(sc, "sc", s_xrk, 16 * i + 9 + ke)
                    W(sc, "sc", s_f1mm, 16 * i + ke + 1)
                    INC(sc.activation(out=blk(rbuf, ke), in_=f1b(ke)[:, 0:N],
                                      func=AF.Sigmoid), "sc", s_f1ev, 1)
                for ke in range(NB):
                    par = ke % 2
                    W(sc, "sc", s_f1mm, 16 * i + 8 + ke + 1)
                    INC(sc.activation(out=half(rk, par), in_=f1b(ke)[:, 0:N],
                                      func=AF.Relu), "sc", s_f1ev, 1)
                    W(sc, "sc", s_f1ev, 24 * i + 8 + 2 * ke + 1)
                    INC(sc.activation(out=blk(kab, ke), in_=half(rk, par), func=AF.Square),
                        "sc", s_f1ev, 1)
                if i >= 1:
                    y2_store(i - 1)
                W(sc, "sc", s_f1mm, 16 * i + 16)
                for ke in range(0, NB, 2):
                    W(sc, "sc", s_z2, 8 * i + ke + 1)
                    INC(sc.activation(out=xr_blk(ke), in_=blk(Z(i), ke), func=AF.Copy),
                        "sc", s_zb2, 1)
                    INC(sc.activation(out=xk_blk(ke), in_=blk(Z(i), ke), func=AF.Square),
                        "sc", s_zb2, 1)
                for ke in range(0, NB, 2):
                    if i + 1 <= NT - 1:
                        W(sc, "sc", s_ov, 8 * i + ke + 1)
                        W(sc, "sc", s_z, 8 * (i + 1) + ke + 1)
                        INC(sc.activation(out=zb1_blk(ke), in_=blk(Z(i + 1), ke),
                                          func=AF.Copy), "sc", s_zb1, 1)
                        INC(sc.activation(out=zq1_blk(ke), in_=blk(Z(i + 1), ke),
                                          func=AF.Square), "sc", s_zb1, 1)
                if i + 2 <= NT - 1:
                    trio_evac(i + 2, 6)
                W(sc, "sc", s_rv, q2)
                iln2 = sc.activation(out=row_t[:, :], in_=row_var[:, :], func=AF.Ln,
                                     bias=consts[0:1, 0:1])
                sfence(sc, iln2)
                INC(sc.activation(out=row_inv[:, :], in_=row_t[:, :], func=AF.Exp,
                                  scale=consts[0:1, 1:2]), "sc", s_rs, 1)
                W(sc, "sc", s_bcast, q2)
                INC(sc.activation(out=dbuf[:, :], in_=psb[:, 0:2 * N], func=AF.Copy),
                    "sc", s_pcp, 1)
                if i + 3 <= NT - 1:
                    j = i + 3
                    W(sc, "sc", s_xst, 32 * (j - 1))
                    W(sc, "sc", s_amm, 24 * (j - 1))
                    INC(sc.activation(out=xbf[:, (j % 2) * 2048:(j % 2) * 2048 + 2048],
                                      in_=stage[:, :], func=AF.Copy), "sc", s_xbf, 1)
                if i + 2 <= NT - 1:
                    trio_evac(i + 2, 7)
            y2_store(NT - 1)
            W(sc, "sc", s_yst, 32 * NT)

        @block.vector
        def _(v):
            W(v, "vec", s_misc, 8)
            W(v, "vec", s_bld, 16)

            def scan_one(ti, ke):
                n = 8 * ti + ke
                W(v, "vec", s_attnv, n + 1)
                W(v, "vec", s_aav, n + 1)
                W(v, "vec", s_cw, n - 7)
                if ti >= 2:
                    W(v, "vec", s_y2, 8 * (ti - 1) - (4 if ke <= 3 else 0))
                INC(v.tensor_tensor_scan(out=blk(Z(ti), ke), data0=blk(aa, ke),
                                           data1=blk(xs, ke),
                                           initial=carry[:, ke:ke + 1],
                                           op0=ALU.mult, op1=ALU.add), "vec", s_scan, 1)
                W(v, "vec", s_scan, n + 1)
                INC(v.tensor_copy(out=carry[:, ke:ke + 1],
                                  in_=bass.AP(Z(ti), ke * N + N - 1, [[2048, 128], [1, 1]])),
                    "vec", s_cw, 1)

            def wb_dst(s):
                w, ke = SLABS[s]
                return wb[w][:, ke * 1024:(ke + 1) * 1024]
            for s in range(NSLAB):
                W(v, "vec", s_wld[s % 9], 32 * (s // 9 + 1))
                INC(v.tensor_copy(out=wb_dst(s), in_=wslot(s)), "vec", s_wcast, 1)
                if s % 3 == 2 and 8 <= s <= 29:
                    scan_one(0, (s - 8) // 3)

            def apply2a(j):
                W(v, "vec", s_pcp, j + 1)
                if j == 0 and NT > 1:
                    W(v, "vec", s_xbf, 2)
                W(v, "vec", s_yst, 32 * j)
                ip2 = v.tensor_tensor(out=z3h(j, 0), in0=z3h(j, 0), in1=Mb_d4,
                                      op=ALU.subtract)
                vfence(v, ip2)
                INC(v.tensor_tensor(out=y2_3a, in0=z3h(j, 0), in1=Sb_d4, op=ALU.mult),
                    "vec", s_y2, 4)

            def apply2b(j):
                ip2 = v.tensor_tensor(out=z3h(j, 1), in0=z3h(j, 1), in1=Mb_d4,
                                      op=ALU.subtract)
                vfence(v, ip2)
                INC(v.tensor_tensor(out=y2_3b, in0=z3h(j, 1), in1=Sb_d4, op=ALU.mult),
                    "vec", s_y2, 4)

            def rows(q):
                W(v, "vec", s_stats, q)
                INC(v.tensor_scalar(out=row_mq[:, :], in0=psb[0:1, 0:2 * N],
                                    scalar1=1.0 / D, scalar2=None, op0=ALU.mult),
                    "vec", s_rpsum, 1)
                W(v, "vec", s_rpsum, q)
                i3 = v.tensor_tensor(out=row_t[:, :], in0=row_mq[0:1, 0:N],
                                     in1=row_mq[0:1, 0:N], op=ALU.mult)
                vfence(v, i3)
                INC(v.tensor_tensor(out=row_var[:, :], in0=row_mq[0:1, N:2 * N],
                                    in1=row_t[:, :], op=ALU.subtract), "vec", s_rv, 1)

            for i in range(NT):
                q1, q2 = 2 * i + 1, 2 * i + 2
                if i >= 1:
                    apply2a(i - 1)
                rows(q1)
                if i + 1 <= NT - 1:
                    scan_one(i + 1, 0)
                    scan_one(i + 1, 1)
                if i >= 1:
                    iyc = v.tensor_copy(out=y1_c0, in_=y1_cN)
                    ordv_n[0] += 1
                    INC(iyc, "vec", s_ordv, 1)
                W(v, "vec", s_bcast, q1)
                ip1 = v.tensor_tensor(out=z3(i), in0=z3(i), in1=Mb_v, op=ALU.subtract)
                vfence(v, ip1)
                if i + 1 <= NT - 1:
                    scan_one(i + 1, 2)
                INC(v.tensor_tensor(out=y1_margin, in0=z3(i), in1=Sb_v, op=ALU.mult),
                    "vec", s_pass, 1)
                W(v, "vec", s_pass, i + 1)
                if i + 1 <= NT - 1:
                    scan_one(i + 1, 3)
                if i == 0:
                    W(v, "vec", s_misc, 16)
                idt = v.tensor_tensor(out=rbuf_blks, in0=y1v_full, in1=xpv_full,
                                      op=ALU.subtract)
                vfence(v, idt)
                for ke in range(NB):
                    INC(v.scalar_tensor_tensor(out=xr_blk(ke), in0=blk(rbuf, ke),
                                               scalar=bpk[:, 24 + ke:25 + ke], in1=xpv(ke),
                                               op0=ALU.mult, op1=ALU.add), "vec", s_xrk, 1)
                if i >= 1:
                    apply2b(i - 1)
                for ke in range(NB):
                    INC(v.scalar_tensor_tensor(out=xk_blk(ke), in0=blk(rbuf, ke),
                                               scalar=bpk[:, 32 + ke:33 + ke], in1=xpv(ke),
                                               op0=ALU.mult, op1=ALU.add), "vec", s_xrk, 1)
                if i + 1 <= NT - 1:
                    for ke in range(4, NB):
                        scan_one(i + 1, ke)
                for ke in range(NB):
                    W(v, "vec", s_f2mm, 8 * i + ke + 1)
                    INC(v.tensor_tensor(out=blk(Z(i), ke), in0=blk(rbuf, ke),
                                        in1=po[ke % 2][:, 0:N], op=ALU.mult), "vec", s_ov, 1)
                    W(v, "vec", s_ov, 8 * i + ke + 1)
                    INC(v.tensor_tensor(out=blk(Z(i), ke), in0=blk(Z(i), ke), in1=y1v(ke),
                                        op=ALU.add), "vec", s_z2, 1)
                rows(q2)
            apply2a(NT - 1)
            apply2b(NT - 1)

        @block.tensor
        def _(t):
            W(t, "ten", s_misc, 8)

            def trio(ti, ke):
                n = 8 * ti + ke
                W(t, "ten", s_wcast, 3 * ke + 3)
                W(t, "ten", s_xbf, ti + 1)
                for gi, (w, bank) in enumerate(((0, pg), (1, pv), (2, pd))):
                    W(t, "ten", s_aevac, 3 * n - 2 + gi)
                    for kd in range(NB):
                        ins = t.matmul(out=bank[:, 0:N], lhsT=wtile(w, ke, kd),
                                       rhs=xbf_blk(ti, kd), start=(kd == 0), stop=(kd == NB - 1))
                        if kd == NB - 1:
                            INC(ins, "ten", s_amm, 1)

            def stats(i, ln, extra):
                q = 2 * i + ln
                W(t, "ten", s_zb1 if ln == 1 else s_zb2, 16 * (i + 1))
                W(t, "ten", s_rpsum, q - 1)
                extra()
                psrc = zz1_pair if ln == 1 else xrk_pair
                for kc in range(NB):
                    ins = t.matmul(out=psb[0:1, 0:2 * N], lhsT=ones_col[:, :],
                                   rhs=psrc(kc), start=(kc == 0), stop=(kc == NB - 1))
                    if kc == NB - 1:
                        INC(ins, "ten", s_stats, 1)

            def bcast(q):
                W(t, "ten", s_rs, q)
                t.matmul(out=psb[:, 0:N], lhsT=ones_rowf[:, :], rhs=row_mq[0:1, 0:N],
                         start=True, stop=True)
                INC(t.matmul(out=psb[:, N:2 * N], lhsT=ones_rowf[:, :], rhs=row_inv[:, :],
                             start=True, stop=True), "ten", s_bcast, 1)

            for ke in range(NB):
                trio(0, ke)
            if NT > 1:
                for ke in range(NB):
                    trio(1, ke)
            for i in range(NT):
                q1, q2 = 2 * i + 1, 2 * i + 2
                has2 = i + 2 <= NT - 1
                stats(i, 1, lambda: W(t, "ten", s_pcp, i))
                if has2:
                    trio(i + 2, 0)
                bcast(q1)
                if has2:
                    for ke in range(1, 6):
                        trio(i + 2, ke)
                W(t, "ten", s_xrk, 16 * i + 8)
                for ke in range(NB):
                    W(t, "ten", s_wcast, 24 + 2 * ke + 1)
                    gate = (24 * i + ke - 1) if ke >= 2 else (24 * i - 3 + 2 * ke)
                    W(t, "ten", s_f1ev, gate)
                    for kc in range(NB):
                        ins = t.matmul(out=f1b(ke)[:, 0:N], lhsT=wtile(3, ke, kc),
                                       rhs=xr_blk(kc), start=(kc == 0), stop=(kc == NB - 1))
                        if kc == NB - 1:
                            INC(ins, "ten", s_f1mm, 1)
                W(t, "ten", s_xrk, 16 * i + 16)
                for ke in range(NB):
                    W(t, "ten", s_wcast, 24 + 2 * ke + 2)
                    gate = (24 * i + 2 * ke + 5) if ke >= 2 else (24 * i + 7 + ke)
                    W(t, "ten", s_f1ev, gate)
                    for kc in range(NB):
                        ins = t.matmul(out=f1b(ke)[:, 0:N], lhsT=wtile(4, ke, kc),
                                       rhs=xk_blk(kc), start=(kc == 0), stop=(kc == NB - 1))
                        if kc == NB - 1:
                            INC(ins, "ten", s_f1mm, 1)
                W(t, "ten", s_f1ev, 24 * i + 24)
                for ke in range(NB):
                    n = 8 * i + ke
                    W(t, "ten", s_wcast, 40 + ke + 1)
                    W(t, "ten", s_ov, n - 1)
                    for kc in range(NB):
                        ins = t.matmul(out=po[ke % 2][:, 0:N], lhsT=wtile(5, ke, kc),
                                       rhs=blk(kab, kc), start=(kc == 0), stop=(kc == NB - 1))
                        if kc == NB - 1:
                            INC(ins, "ten", s_f2mm, 1)
                if has2:
                    trio(i + 2, 6)
                stats(i, 2, lambda: W(t, "ten", s_pass, i + 1))
                if has2:
                    trio(i + 2, 7)
                for _ in range(20):
                    t.matmul(out=pkr[0:1, 0:N], lhsT=ones_col[:, :],
                             rhs=xbf[:, 0:N], start=True, stop=True)
                bcast(q2)
    return nc


def slab_lin(w):
    return np.ascontiguousarray(
        np.asarray(w, np.float32).reshape(8, 128, 8, 128)
        .transpose(1, 2, 0, 3).reshape(128, 8192))


def kernel(L0_attn_x, L0_attn_Wg, L0_attn_bg, L0_attn_Wv, L0_attn_bv, L0_attn_Wd,
           L0_attn_bd, L0_ffn_W_r, L0_ffn_W_k, L0_ffn_W_v, L0_ffn_mu_r, L0_ffn_mu_k):
    x = np.asarray(L0_attn_x, dtype=np.float32)
    B = x.shape[0]
    packs = [np.asarray(a, np.float32).reshape(NB, 128).T
             for a in (L0_attn_bg, L0_attn_bv, L0_attn_bd, L0_ffn_mu_r, L0_ffn_mu_k)]
    bpk = np.ascontiguousarray(np.concatenate(packs, axis=1))
    shared = {
        "wg": slab_lin(L0_attn_Wg),
        "wv": slab_lin(L0_attn_Wv),
        "wd": slab_lin(L0_attn_Wd),
        "wr": slab_lin(np.asarray(L0_ffn_W_r, np.float32).T),
        "wk": slab_lin(np.asarray(L0_ffn_W_k, np.float32).T),
        "wvf": slab_lin(np.asarray(L0_ffn_W_v, np.float32).T),
        "bpk": bpk,
    }
    nc = build_nc()
    in_maps = []
    for b in range(B):
        m = dict(shared)
        m["xT"] = np.ascontiguousarray(x[b].T)
        in_maps.append(m)
    res = run_bass_kernel_spmd(nc, in_maps, core_ids=list(range(B)))
    out = np.empty((B, SEQ, D), dtype=np.float32)
    for b in range(B):
        out[b] = res.results[b]["yT"].T
    return out
